# version 01 (final)
# baseline (speedup 1.0000x reference)
import numpy as np
from contextlib import ExitStack

import concourse.bass as bass
import concourse.mybir as mybir
from concourse.bass_utils import run_bass_kernel_spmd

F32 = mybir.dt.float32
AL = mybir.AluOpType
AF = mybir.ActivationFunctionType
AX = mybir.AxisListType.X

T = 2048
D = 1024
NCH = 8
TB = 512
NTB = T // TB
TC = 128
NTC = T // TC
G = 128
NHEADS = 32
HP = 16
AHP = 8
D2P = 2816
K2N = 22
UCOLS = T + 32
WIN = 256

CFG = dict(layers=6, mamba=True, ln_only=False)

ENGINES = ("tensor", "vector", "scalar", "gpsimd", "sync")
ESEM = {e: "s_" + e for e in ENGINES}


def _f32(x):
    return np.ascontiguousarray(x, dtype=np.float32)


def host_prepare(inputs):
    d = {}
    di, nh, ds = 2048, NHEADS, 64
    W_in = np.asarray(inputs["L0_attn_W_in"])
    rows = [np.arange(0, di), np.arange(di, 2 * di)]
    base = 2 * di
    bidx = np.stack([base + h * (2 * ds + 1) + np.arange(ds) for h in range(nh)])
    cidx = np.stack([base + h * (2 * ds + 1) + ds + np.arange(ds) for h in range(nh)])
    didx = np.array([base + h * (2 * ds + 1) + 2 * ds for h in range(nh)])
    perm = np.concatenate(rows + [bidx.reshape(-1), cidx.reshape(-1), didx])
    WinT = W_in[perm].T
    WinTp = np.zeros((D, 65 * 128), np.float32)
    WinTp[:, :8224] = WinT
    tl = [WinTp[k * 128:(k + 1) * 128, m * 128:(m + 1) * 128]
          for m in range(65) for k in range(8)]
    d["s_win"] = _f32(np.concatenate(tl, axis=1))

    WoutT = np.asarray(inputs["L0_attn_W_out"]).T
    tl = [WoutT[k * 128:(k + 1) * 128, m * 128:(m + 1) * 128]
          for m in range(8) for k in range(16)]
    d["s_wout"] = _f32(np.concatenate(tl, axis=1))

    Wqkv = np.asarray(inputs["L0_ffn_Wqkv"])
    Wo = np.asarray(inputs["L0_ffn_Wo"])
    W1 = np.asarray(inputs["L0_ffn_W1"])
    W2 = np.asarray(inputs["L0_ffn_W2"])
    for i in range(6):
        tl = []
        for hp in range(AHP):
            for mcol in (hp * 128, 1024 + hp * 128, 2048 + hp * 128):
                for k in range(8):
                    tl.append(Wqkv[i, k * 128:(k + 1) * 128, mcol:mcol + 128])
        d[f"s_qkv{i}"] = _f32(np.concatenate(tl, axis=1))
        tl = [Wo[i, k * 128:(k + 1) * 128, m * 128:(m + 1) * 128]
              for m in range(8) for k in range(8)]
        d[f"s_wo{i}"] = _f32(np.concatenate(tl, axis=1))
        W1a = np.zeros((D, D2P), np.float32); W1a[:, :2730] = W1[i, :, :2730]
        W1g = np.zeros((D, D2P), np.float32); W1g[:, :2730] = W1[i, :, 2730:]
        tl = []
        for k2 in range(K2N):
            for half in (W1a, W1g):
                for k in range(8):
                    tl.append(half[k * 128:(k + 1) * 128, k2 * 128:(k2 + 1) * 128])
        d[f"s_w1{i}"] = _f32(np.concatenate(tl, axis=1))
        W2p = np.zeros((D2P, D), np.float32); W2p[:2730] = W2[i]
        tl = [W2p[k2 * 128:(k2 + 1) * 128, m * 128:(m + 1) * 128]
              for k2 in range(K2N) for m in range(8)]
        d[f"s_w2{i}"] = _f32(np.concatenate(tl, axis=1))

    conv_w = np.asarray(inputs["L0_attn_conv_w"])[:, 0, :]
    d["c_convw"] = _f32(conv_w.reshape(16, 128, 4).transpose(1, 0, 2).reshape(128, 64))
    d["c_convb"] = _f32(np.asarray(inputs["L0_attn_conv_b"]).reshape(16, 128).T)
    d["c_alog"] = _f32(np.asarray(inputs["L0_attn_A_log"]).reshape(16, 128).T)
    d["c_alog2"] = _f32(np.asarray(inputs["L0_attn_A_log"]))
    d["c_dcol"] = _f32(np.repeat(np.asarray(inputs["L0_attn_D"]), 64).reshape(16, 128).T)
    d["c_dtb"] = _f32(np.asarray(inputs["L0_attn_dt_bias"]).reshape(32, 1))
    d["c_qs"] = _f32(np.tile(np.asarray(inputs["L0_ffn_q_scale"]), (1, 2)).T)
    d["c_ks"] = _f32(np.tile(np.asarray(inputs["L0_ffn_k_scale"]), (1, 2)).T)
    d["c_lng"] = _f32(np.asarray(inputs["L0_ffn_ln_g"]).reshape(6, 8, 128)
                      .transpose(2, 0, 1).reshape(128, 48))
    d["c_lnb"] = _f32(np.asarray(inputs["L0_ffn_ln_b"]).reshape(6, 8, 128)
                      .transpose(2, 0, 1).reshape(128, 48))
    b1 = np.asarray(inputs["L0_ffn_b1"])
    cols = np.zeros((128, 6, K2N, 2), np.float32)
    for i in range(6):
        for k2 in range(K2N):
            sa = b1[i, k2 * 128:min((k2 + 1) * 128, 2730)]
            sg = b1[i, 2730 + k2 * 128:2730 + min((k2 + 1) * 128, 2730)]
            cols[:len(sa), i, k2, 0] = sa
            cols[:len(sg), i, k2, 1] = sg
    d["c_b1"] = _f32(cols.reshape(128, 6 * K2N * 2))
    d["c_b2"] = _f32(np.asarray(inputs["L0_ffn_b2"]).reshape(6, 8, 128)
                     .transpose(2, 0, 1).reshape(128, 48))

    d["c_ident"] = np.eye(128, dtype=np.float32)
    Pr = np.zeros((128, 128), np.float32)
    for h in range(2):
        o = h * 64
        for j in range(32):
            Pr[o + 32 + j, o + j] = -1.0
            Pr[o + j, o + 32 + j] = 1.0
    d["c_prot"] = Pr
    d["c_tri"] = _f32(np.triu(np.ones((128, 128), np.float32)))
    exp = np.zeros((32, HP * 128), np.float32)
    for hp in range(HP):
        exp[2 * hp, hp * 128:hp * 128 + 64] = 1.0
        exp[2 * hp + 1, hp * 128 + 64:hp * 128 + 128] = 1.0
    d["c_expander"] = exp
    d["c_meancol"] = np.full((128, 1), 1.0 / 1024.0, np.float32)
    ones2 = np.zeros((128, 2), np.float32)
    ones2[0:64, 0] = 1.0
    ones2[64:128, 1] = 1.0
    d["c_ones2"] = ones2
    bc2 = np.zeros((2, 128), np.float32)
    bc2[0, 0:64] = 1.0
    bc2[1, 64:128] = 1.0
    d["c_bc2"] = bc2
    d["c_ones1r"] = np.ones((1, 128), np.float32)
    d["c_zrow"] = np.zeros((1, 128), np.float32)
    bsel = np.zeros((16, 8 * 128), np.float32)
    for dl in range(8):
        for h in range(2):
            bsel[dl * 2 + h, dl * 128 + h * 64:dl * 128 + h * 64 + 64] = 1.0
    d["c_bsel"] = _f32(bsel)
    m4 = np.zeros((16, 128), np.float32)
    for dl in range(4):
        for h in range(2):
            m4[dl * 2 + h, :] = (np.arange(128) % 4 >= dl).astype(np.float32)
    d["c_m4"] = _f32(m4)

    NEG = np.float32(-1.0e30)
    m0 = np.full((128, 512), NEG, np.float32)
    m1 = np.full((128, 512), NEG, np.float32)
    for r in range(128):
        m0[r, r:r + WIN + 1] = 0.0
        m1[r, 128 + r:128 + r + WIN + 1] = 0.0
    c0 = np.full((128, 256), NEG, np.float32)
    c1 = np.full((128, 256), NEG, np.float32)
    for r in range(128):
        c0[r, :r + 1] = 0.0
        c1[r, :128 + r + 1] = 0.0
    d["c_masks"] = _f32(np.concatenate([m0, m1, c0, c1], axis=1))

    half = np.arange(0, 64, 2, dtype=np.float32)
    inv_freq = (1.0 / (10000.0 ** (half / np.float32(64.0)))).astype(np.float32)
    t = np.arange(T, dtype=np.float32)
    freqs = (t[:, None] * inv_freq[None, :]).astype(np.float32)
    freqs = np.concatenate([freqs, freqs], axis=1)
    base_scale = ((half + 0.4 * 64.0) / (1.4 * 64.0)).astype(np.float32)
    power = ((t - T // 2) / (WIN // 2)).astype(np.float32)
    scale = (base_scale[None, :] ** power[:, None]).astype(np.float32)
    scale = np.concatenate([scale, scale], axis=1)
    cos = np.cos(freqs).astype(np.float32)
    sin = np.sin(freqs).astype(np.float32)
    inv = (np.float32(1.0) / scale).astype(np.float32)
    tab = np.zeros((128, 4 * T), np.float32)
    for j, tb_ in enumerate(((cos * scale).T, (sin * scale).T,
                             (cos * inv).T, (sin * inv).T)):
        tab[0:64, j * T:(j + 1) * T] = tb_.astype(np.float32)
        tab[64:128, j * T:(j + 1) * T] = tb_.astype(np.float32)
    d["c_tables"] = _f32(tab)
    return d


CBUCK = 512
PBUCK = 32


def _ap_rect(ap):
    t = ap.tensor
    name = t.name
    shape = list(t.shape)
    ncols = 1 if len(shape) < 2 else int(np.prod(shape[1:]))
    dims = list(ap.ap)
    off = ap.offset
    p0 = off // ncols if ncols else 0
    c0 = off % ncols if ncols else 0
    pcount = dims[0][1] if dims else 1
    span = 1
    for step, count in dims[1:]:
        span += abs(step) * (count - 1)
    return (name, p0, p0 + pcount, c0, c0 + span)


class Kern:
    def __init__(self, cfg, shapes):
        self.cfg = cfg
        nc = bass.Bass("TRN2", target_bir_lowering=False)
        self.nc = nc
        self.es = ExitStack()
        self.streams = {e: [] for e in ENGINES}
        self.sems = {}
        self.counts = {}
        self.waited = {e: {} for e in ENGINES}
        self.track = {}
        semnames = ["s_tensor", "s_vector", "s_scalar", "s_gpsimd", "s_sync",
                    "b_tensor", "b_vector", "b_scalar", "b_gpsimd", "b_sync"]
        self.QK = {"s_w": 8, "s_ing": 4, "s_ins": 8, "s_out": 6}
        for q, k in self.QK.items():
            semnames += [f"{q}{j}" for j in range(k)]
        for s in semnames:
            self.sems[s] = self.es.enter_context(nc.semaphore(s))
            self.counts[s] = 0
        self.qn = {q: 0 for q in self.QK}

        self.dr = {}
        self.dr["x"] = nc.dram_tensor("x", [D, T], F32, kind="ExternalInput").ap()
        self.dr["out"] = nc.dram_tensor("out", [D, T], F32, kind="ExternalOutput").ap()
        self.dr["acc"] = nc.dram_tensor("acc_scratch", [D, T], F32, kind="Internal").ap()
        for name, shp in shapes.items():
            self.dr[name] = nc.dram_tensor(name, list(shp), F32,
                                           kind="ExternalInput").ap()

        def sb(name, cols, parts=128):
            return self.es.enter_context(nc.sbuf_tensor(name, [parts, cols], F32))
        self.u = sb("u", NCH * UCOLS)
        self.wsb = sb("wring", 48 * 128)
        self.Hst = sb("Hst", HP * 128)
        self.cst = {}
        for name, cols, parts in [
                ("convw", 64, 128), ("convb", 16, 128), ("alog", 16, 128),
                ("dcol", 16, 128), ("dtb", 1, 32), ("qs", 6, 128), ("ks", 6, 128),
                ("lng", 48, 128), ("lnb", 48, 128), ("b1", 6 * K2N * 2, 128),
                ("b2", 48, 128), ("ident", 128, 128), ("prot", 128, 128),
                ("tri", 128, 128), ("expander", HP * 128, 32), ("meancol", 1, 128),
                ("ones2", 2, 128), ("bc2", 128, 2), ("ones1r", 128, 1),
                ("zrow", 128, 1), ("bsel", 8 * 128, 16), ("m4", 128, 16),
                ("alog2", 64, 32)]:
            self.cst[name] = sb("k_" + name, cols, parts)
        self.acol = sb("acol", 16)
        self.AR_COLS = 23040
        self.arena = sb("arena", self.AR_COLS)

        self.banks = []
        for i in range(8):
            t = self.es.enter_context(nc.psum_tensor(f"psb{i}", [128, 512], F32))
            self.banks.append(t)
        self.rr = 0

        self.GRP = 8
        self.NSLOT = 48
        self.w_next_tile = 0
        self.w_next_dma = 0
        self.w_segments = []

    def _wait(self, eng, tok):
        if tok is None:
            return
        sname, val = tok
        assert val <= self.counts[sname], (eng, sname, val, self.counts[sname])
        w = self.waited[eng]
        if w.get(sname, -1) >= val:
            return
        w[sname] = val
        h = self.sems[sname]
        self.streams[eng].append(lambda e, h=h, v=val: e.wait_ge(h, v))

    def _buckets(self, rect):
        name, p0, p1, c0, c1 = rect
        m = self.track.setdefault(name, {})
        out = []
        for pq in range(p0 // PBUCK, (p1 - 1) // PBUCK + 1):
            for cq in range(c0 // CBUCK, (c1 - 1) // CBUCK + 1):
                out.append(m.setdefault((pq, cq), {"w": None, "r": {}}))
        return out

    def _pre(self, eng, reads, writes):
        for ap in reads:
            for b in self._buckets(_ap_rect(ap)):
                self._wait(eng, b["w"])
        for ap in writes:
            for b in self._buckets(_ap_rect(ap)):
                self._wait(eng, b["w"])
                for tok in b["r"].values():
                    self._wait(eng, tok)

    def _post(self, eng, reads, writes, tok):
        for ap in reads:
            for b in self._buckets(_ap_rect(ap)):
                pr = b["r"].get(eng)
                if pr is None or pr[0] != tok[0] or pr[1] < tok[1]:
                    b["r"][eng] = tok
        for ap in writes:
            for b in self._buckets(_ap_rect(ap)):
                b["w"] = tok
                b["r"] = {}

    def op(self, eng, fn, r=(), w=()):
        self._pre(eng, r, w)
        sname = ESEM[eng]
        self.counts[sname] += 1
        h = self.sems[sname]
        self.streams[eng].append(lambda e, fn=fn, h=h: fn(e).then_inc(h, 1))
        tok = (sname, self.counts[sname])
        self._post(eng, r, w, tok)
        return tok

    def dma(self, qeng, queue, out_ap, in_ap):
        n = self.qn[queue]
        self.qn[queue] = n + 1
        k = self.QK[queue]
        sname = f"{queue}{n % k}"
        if n >= k:
            self._wait(qeng, (sname, 16 * (n // k)))
        self._pre(qeng, [in_ap], [out_ap])
        self.counts[sname] += 16
        h = self.sems[sname]
        self.streams[qeng].append(
            lambda e, o=out_ap, i=in_ap, h=h: e.dma_start(out=o, in_=i)
            .then_inc(h, 16))
        tok = (sname, self.counts[sname])
        self._post(qeng, [in_ap], [out_ap], tok)
        return tok

    def tok(self, sem):
        return (sem, self.counts[sem])

    def barrier(self):
        toks = {}
        for e in ENGINES:
            s = "b_" + e
            self.counts[s] += 1
            h = self.sems[s]
            self.streams[e].append(lambda en, h=h: en.nop().then_inc(h, 1))
            toks[e] = (s, self.counts[s])
        for e in ENGINES:
            for e2 in ENGINES:
                if e2 != e:
                    self._wait(e, toks[e2])

    def vop(self, fn, r=(), w=()):
        return self.op("vector", fn, r, w)

    def sop(self, fn, r=(), w=()):
        return self.op("scalar", fn, r, w)

    def gop(self, fn, r=(), w=()):
        return self.op("gpsimd", fn, r, w)

    def mm(self, out_ap, lhsT, rhs, start, stop, tp=None, skip_group=False):
        eng = "tensor"
        self._pre(eng, [lhsT, rhs], [out_ap] if start else [])
        sname = "s_tensor"
        self.counts[sname] += 1
        h = self.sems[sname]
        self.streams[eng].append(
            lambda e, o=out_ap, l=lhsT, rr=rhs, s1=start, s2=stop, h=h, t=tp,
            sg=skip_group: e.matmul(o, l, rr, start=s1, stop=s2,
                                    tile_position=t, skip_group_check=sg)
            .then_inc(h, 1))
        tok = (sname, self.counts[sname])
        self._post(eng, [lhsT, rhs], [out_ap] if stop else [], tok)
        return tok

    def transpose(self, out_ap, in_ap):
        return self.op("tensor",
                       lambda e, o=out_ap, i=in_ap:
                       e.transpose(o, i, self.cst["ident"][:]),
                       r=[in_ap, self.cst["ident"][:]], w=[out_ap])

    def ar(self, off, cols):
        assert off + cols <= self.AR_COLS, (off, cols)
        return self.arena[:, off:off + cols]

    def bank(self):
        t = self.banks[self.rr % 8]
        self.rr += 1
        return t

    def w_add_segment(self, dram_ap, ntiles):
        self.w_segments.append((dram_ap, ntiles))

    def w_total(self):
        return sum(s[1] for s in self.w_segments)

    def _w_emit_group(self):
        g = self.w_next_dma
        tile = g * self.GRP
        remaining = self.GRP
        col = (tile % self.NSLOT) * 128
        pos = 0
        for dram_ap, nt in self.w_segments:
            if tile < pos + nt and remaining > 0:
                off = tile - pos
                take = min(nt - off, remaining)
                self.dma("sync", "s_w",
                         self.wsb[:, col:col + take * 128],
                         dram_ap[:, off * 128:(off + take) * 128])
                tile += take
                col += take * 128
                remaining -= take
            pos += nt
            if remaining == 0:
                break
        assert remaining == 0, "weight ring: out of segments"
        self.w_next_dma += 1

    def wtile(self, prefetch=4):
        i = self.w_next_tile
        g = i // self.GRP
        last_g = (self.w_total() - 1) // self.GRP
        while self.w_next_dma <= min(g + prefetch, last_g):
            self._w_emit_group()
        slot = i % self.NSLOT
        self.w_next_tile += 1
        return self.wsb[:, slot * 128:(slot + 1) * 128]

    def pe_group(self, bank_ap, pairs):
        n = len(pairs)
        tok = None
        for i, (l, r) in enumerate(pairs):
            tok = self.mm(bank_ap, l, r, start=(i == 0), stop=(i == n - 1))
        return tok

    def ucol(self, ch, t0, n, shift=False):
        off = ch * UCOLS + t0 + (0 if (shift and ch >= 4) else 1)
        return self.u[:, off:off + n]

    def build(self):
        cfg = self.cfg
        self.load_consts()
        self.register_weight_segments()
        self.p0_layernorm()
        self.barrier()
        if cfg["mamba"] and not cfg["ln_only"]:
            self.p1_mamba()
            self.barrier()
        if cfg["layers"] > 0 and not cfg["ln_only"]:
            self.p2_ffn()
            self.barrier()
        self.p3_final()
        block = self.es.enter_context(self.nc.Block())
        for ename in ENGINES:
            stream = self.streams[ename]

            def make(stream):
                def f(e):
                    for fn in stream:
                        fn(e)
                return f
            getattr(block, ename)(make(stream))
        self.es.close()
        return self.nc

    def register_weight_segments(self):
        cfg = self.cfg
        if cfg["ln_only"]:
            return
        if cfg["mamba"]:
            for tc in range(NTC):
                self.w_add_segment(self.dr["s_win"], 520)
                self.w_add_segment(self.dr["s_wout"], 128)
        for l in range(cfg["layers"]):
            for tb in range(NTB):
                self.w_add_segment(self.dr[f"s_qkv{l}"], 192)
                self.w_add_segment(self.dr[f"s_wo{l}"], 64)
            for tb in range(NTB):
                self.w_add_segment(self.dr[f"s_w1{l}"], 352)
                self.w_add_segment(self.dr[f"s_w2{l}"], 176)

    def load_consts(self):
        self.vop(lambda e: e.memset(self.arena[:], 0.0), w=[self.arena[:]])
        self.vop(lambda e: e.memset(self.u[:], 0.0), w=[self.u[:]])
        for name in self.cst:
            self.dma("gpsimd", "s_ing", self.cst[name][:], self.dr["c_" + name][:, :])
        self.sop(lambda e: e.activation(self.acol[:], self.cst["alog"][:], AF.Exp,
                                        bias=0.0, scale=1.0),
                 r=[self.cst["alog"][:]], w=[self.acol[:]])
        self.sop(lambda e: e.mul(self.acol[:], self.acol[:], -1.0),
                 r=[self.acol[:]], w=[self.acol[:]])
        self.vop(lambda e: e.tensor_scalar(self.cst["qs"][:], self.cst["qs"][:],
                                           8.0, None, AL.mult),
                 r=[self.cst["qs"][:]], w=[self.cst["qs"][:]])

        uap = self.u[:].rearrange("p (c n) -> p c n", n=UCOLS)[:, :, 0:1]
        self.vop(lambda e: e.memset(uap, 0.0), w=[uap])

    def ln_stats(self, read_fn, mrow, vrow, sq, eps_col):
        bmean = self.bank()
        bsq = self.bank()
        for ch in range(NCH):
            self.mm(bmean[0:1, 0:TB], self.cst["meancol"][:], read_fn(ch),
                    start=(ch == 0), stop=(ch == NCH - 1))
        for ch in range(NCH):
            rap = read_fn(ch)
            self.sop(lambda e, c=ch, a_=rap: e.activation(
                sq[:, c * TB:(c + 1) * TB], a_, AF.Square,
                bias=0.0, scale=1.0),
                r=[rap], w=[sq[:, ch * TB:(ch + 1) * TB]])
        for ch in range(NCH):
            self.mm(bsq[0:1, 0:TB], self.cst["meancol"][:],
                    sq[:, ch * TB:(ch + 1) * TB],
                    start=(ch == 0), stop=(ch == NCH - 1))
        self.vop(lambda e: e.tensor_copy(mrow[0:1, :], bmean[0:1, 0:TB]),
                 r=[bmean[0:1, 0:TB]], w=[mrow[0:1, :]])
        self.vop(lambda e: e.tensor_tensor(vrow[0:1, :], mrow[0:1, :],
                                           mrow[0:1, :], AL.mult),
                 r=[mrow[0:1, :]], w=[vrow[0:1, :]])
        self.vop(lambda e: e.scalar_tensor_tensor(
            vrow[0:1, :], vrow[0:1, :], -1.0, bsq[0:1, 0:TB], AL.mult, AL.add),
            r=[vrow[0:1, :], bsq[0:1, 0:TB]], w=[vrow[0:1, :]])
        self.sop(lambda e: e.activation(vrow[0:1, :], vrow[0:1, :], AF.Sqrt,
                                        bias=eps_col, scale=1.0),
                 r=[vrow[0:1, :], eps_col], w=[vrow[0:1, :]])
        self.vop(lambda e: e.reciprocal(vrow[0:1, :], vrow[0:1, :]),
                 r=[vrow[0:1, :]], w=[vrow[0:1, :]])
        bmu = self.bank()
        brs = self.bank()
        self.mm(bmu[:, 0:TB], self.cst["ones1r"][:], mrow[0:1, :],
                start=True, stop=True)
        self.mm(brs[:, 0:TB], self.cst["ones1r"][:], vrow[0:1, :],
                start=True, stop=True)
        return bmu, brs

    def p0_layernorm(self):
        xstage = self.ar(0, NCH * TB)
        sq = self.ar(NCH * TB, NCH * TB)
        mrow = self.ar(2 * NCH * TB, TB)
        vrow = self.ar(2 * NCH * TB + TB, TB)
        epsc = self.ar(2 * NCH * TB + 2 * TB, 1)
        self.vop(lambda e: e.memset(epsc[0:1, :], 1e-5), w=[epsc[0:1, :]])
        for tt in range(NTB):
            for ch in range(NCH):
                self.dma("scalar", "s_ins", xstage[:, ch * TB:(ch + 1) * TB],
                         self.dr["x"][ch * 128:(ch + 1) * 128,
                                      tt * TB:(tt + 1) * TB])
            bmu, brs = self.ln_stats(
                lambda ch: xstage[:, ch * TB:(ch + 1) * TB], mrow, vrow, sq,
                epsc[0:1, 0:1])
            for ch in range(NCH):
                dst = self.ucol(ch, tt * TB, TB)
                src = xstage[:, ch * TB:(ch + 1) * TB]
                self.vop(lambda e, d=dst, s=src, mu=bmu: e.tensor_tensor(
                    d, s, mu[:, 0:TB], AL.subtract),
                    r=[src, bmu[:, 0:TB]], w=[dst])
                self.vop(lambda e, d=dst, rs=brs: e.tensor_tensor(
                    d, d, rs[:, 0:TB], AL.mult),
                    r=[dst, brs[:, 0:TB]], w=[dst])

    def p1_mamba(self):
        o = [0]

        def take(n):
            r = o[0]
            o[0] += n
            return r
        ST = 3 + TC
        a_xin = take(16 * ST)
        a_z = take(16 * TC)
        a_B = take(16 * TC)
        a_C = take(16 * TC)
        a_W = take(16 * TC)
        a_tmp = take(16 * TC)
        a_xh = take(16 * TC)
        a_y = take(16 * TC)
        a_dt = take(TC)
        a_e = take(TC)
        a_R = take(HP)
        a_xT = take(4 * G)
        a_BT = take(G)
        a_M = take(8 * G)
        a_cf = take(4)
        a_CL = take(G)
        a_C2 = take(G)
        a_Btil = take(G)
        a_BL = take(G)
        a_band = take(G)
        a_mrow = take(G)
        a_tail = take(16 * 4)
        a_xacc = take(TC)
        AR = self.ar

        tails = AR(a_tail, 16 * 4)
        self.vop(lambda e: e.memset(tails, 0.0), w=[tails])
        self.vop(lambda e: e.memset(self.Hst[:], 0.0), w=[self.Hst[:]])

        for tc in range(NTC):
            t0 = tc * TC
            for m in range(65):
                bk = self.bank()
                pairs = []
                for k in range(8):
                    wt = self.wtile()
                    pairs.append((wt, self.ucol(k, t0, TC)))
                self.pe_group(bk[:, 0:TC], pairs)
                p = bk[:, 0:TC]
                if m < 16:
                    dst = AR(a_xin + m * ST + 3, TC)
                    self.vop(lambda e, d=dst, p=p: e.tensor_copy(d, p),
                             r=[p], w=[dst])
                elif m < 32:
                    dst = AR(a_z + (m - 16) * TC, TC)
                    self.sop(lambda e, d=dst, p=p: e.activation(
                        d, p, AF.Silu, bias=0.0, scale=1.0), r=[p], w=[dst])
                elif m < 48:
                    dst = AR(a_B + (m - 32) * TC, TC)
                    self.vop(lambda e, d=dst, p=p: e.tensor_copy(d, p),
                             r=[p], w=[dst])
                elif m < 64:
                    dst = AR(a_C + (m - 48) * TC, TC)
                    self.sop(lambda e, d=dst, p=p: e.copy(d, p), r=[p], w=[dst])
                else:
                    dst = AR(a_dt, TC)
                    self.vop(lambda e, d=dst, p=bk:
                             e.tensor_copy(d[0:32, :], p[0:32, 0:TC]),
                             r=[bk[0:32, 0:TC]], w=[dst[0:32, :]])

            for ch in range(16):
                st = AR(a_xin + ch * ST, ST)
                tail = AR(a_tail + ch * 4, 4)
                dst = AR(a_xh + ch * TC, TC)
                self.vop(lambda e, s=st, t=tail: e.tensor_copy(s[:, 0:3],
                                                               t[:, 0:3]),
                         r=[tail[:, 0:3]], w=[st[:, 0:3]])
                wc0 = self.cst["convw"][:, ch * 4:ch * 4 + 1]
                self.vop(lambda e, d=dst, s=st, w_=wc0:
                         e.tensor_scalar(d, s[:, 0:TC], w_, None, AL.mult),
                         r=[st[:, 0:TC], wc0], w=[dst])
                for kk in range(1, 4):
                    wck = self.cst["convw"][:, ch * 4 + kk:ch * 4 + kk + 1]
                    self.vop(lambda e, d=dst, s=st, k=kk, w_=wck:
                             e.scalar_tensor_tensor(d, s[:, k:k + TC], w_, d,
                                                    AL.mult, AL.add),
                             r=[st[:, kk:kk + TC], dst, wck], w=[dst])
                self.vop(lambda e, s=st, t=tail:
                         e.tensor_copy(t[:, 0:3], s[:, TC:TC + 3]),
                         r=[st[:, TC:TC + 3]], w=[tail[:, 0:3]])
                bc = self.cst["convb"][:, ch:ch + 1]
                self.sop(lambda e, d=dst, bc=bc:
                         e.activation(d, d, AF.Silu, bias=bc, scale=1.0),
                         r=[dst, bc], w=[dst])

            dtr = AR(a_dt, TC)
            er = AR(a_e, TC)
            self.sop(lambda e: e.activation(dtr[0:32, :], dtr[0:32, :], AF.Exp,
                                            bias=self.cst["dtb"][0:32, 0:1],
                                            scale=1.0),
                     r=[dtr[0:32, :], self.cst["dtb"][0:32, 0:1]],
                     w=[dtr[0:32, :]])
            self.sop(lambda e: e.activation(dtr[0:32, :], dtr[0:32, :], AF.Ln,
                                            bias=1.0, scale=1.0),
                     r=[dtr[0:32, :]], w=[dtr[0:32, :]])
            for c in range(TC // G):
                self.vop(lambda e, c=c: e.tensor_tensor_scan(
                    er[0:32, c * G:(c + 1) * G], dtr[0:32, c * G:(c + 1) * G],
                    dtr[0:32, c * G:(c + 1) * G], 0.0, AL.add, AL.bypass),
                    r=[dtr[0:32, c * G:(c + 1) * G]],
                    w=[er[0:32, c * G:(c + 1) * G]])

            for c in range(TC // G):
                cc = c * G
                erc = er[0:32, cc:cc + G]
                for hp in range(HP):
                    exp_st = self.cst["expander"][:, hp * 128:(hp + 1) * 128]
                    Ahp = self.acol[:, hp:hp + 1]
                    Hsl = self.Hst[:, hp * 128:(hp + 1) * 128]
                    Wsl = AR(a_W + hp * TC + cc, G)
                    Bsl = AR(a_B + hp * TC + cc, G)
                    Csl = AR(a_C + hp * TC + cc, G)
                    tmp = AR(a_tmp + hp * TC + cc, G)
                    xh_sl = AR(a_xh + hp * TC + cc, G)
                    cf = AR(a_cf, 4)
                    C2 = AR(a_C2, G)
                    Btl = AR(a_Btil, G)
                    CL = AR(a_CL, G)
                    BL = AR(a_BL, G)
                    bke = self.bank()
                    self.mm(bke[:, 0:G], exp_st, erc, start=True, stop=True)
                    self.vop(lambda e, W=Wsl, p=bke, A=Ahp:
                             e.tensor_scalar(W, p[:, 0:G], A, None, AL.mult),
                             r=[bke[:, 0:G], Ahp], w=[Wsl])
                    bkd = self.bank()
                    self.mm(bkd[:, 0:G], exp_st, dtr[0:32, cc:cc + G],
                            start=True, stop=True)
                    self.vop(lambda e, B=Bsl, p=bkd:
                             e.tensor_tensor(B, B, p[:, 0:G], AL.mult),
                             r=[Bsl, bkd[:, 0:G]], w=[Bsl])
                    self.sop(lambda e, t=tmp, W=Wsl: e.activation(
                        t, W, AF.Exp, bias=0.0, scale=1.0), r=[Wsl], w=[tmp])
                    self.vop(lambda e, d=C2, C=Csl, t=tmp:
                             e.tensor_tensor(d, C, t, AL.mult),
                             r=[Csl, tmp], w=[C2])
                    self.sop(lambda e, c2=cf, W=Wsl: e.activation(
                        c2[:, 2:3], W[:, G - 1:G], AF.Exp, bias=0.0, scale=1.0),
                        r=[Wsl[:, G - 1:G]], w=[cf[:, 2:3]])
                    self.vop(lambda e, t=tmp, W=Wsl: e.tensor_scalar(
                        t, W, W[:, G - 1:G], None, AL.subtract),
                        r=[Wsl], w=[tmp])
                    self.sop(lambda e, t=tmp: e.activation(
                        t, t, AF.Exp, bias=0.0, scale=-1.0), r=[tmp], w=[tmp])
                    self.vop(lambda e, d=Btl, B=Bsl, t=tmp:
                             e.tensor_tensor(d, B, t, AL.mult),
                             r=[Bsl, tmp], w=[Btl])
                    xTs = {}
                    for s in (0, 8, 16, 24):
                        bkT = self.bank()
                        src_ap = AR(a_xh + hp * TC + cc + s, G)
                        self.transpose(bkT[:, 0:G], src_ap)
                        xTs[s] = AR(a_xT + (s // 8) * G, G)
                        self.vop(lambda e, d=xTs[s], p=bkT:
                                 e.tensor_copy(d, p[:, 0:G]),
                                 r=[bkT[:, 0:G]], w=[xTs[s]])
                    bkT2 = self.bank()
                    self.transpose(bkT2[:, 0:G], Btl)
                    BT = AR(a_BT, G)
                    self.vop(lambda e, d=BT, p=bkT2: e.tensor_copy(d, p[:, 0:G]),
                             r=[bkT2[:, 0:G]], w=[BT])
                    bkM = [self.bank(), self.bank()]
                    for h in range(2):
                        self.mm(bkM[h][:, 0:512], self.cst["zrow"][:],
                                self.u[0:1, 0:512], start=True, stop=True)
                    for L in (128, 64, 32, 16, 8):
                        wref = AR(a_mrow, G)
                        rap = Wsl[:].rearrange("p (b l) -> p b l", l=L)[
                            :, :, L // 2 - 1:L // 2]
                        wap = wref[:].rearrange("p (b l) -> p b l", l=L)
                        bap = rap.broadcast_to([128, G // L, L])
                        self.vop(lambda e, d=wap, s_=bap: e.tensor_copy(d, s_),
                                 r=[Wsl], w=[wref])
                        self.vop(lambda e, t=tmp, W=Wsl, w_=wref:
                                 e.tensor_tensor(t, W, w_, AL.subtract),
                                 r=[Wsl, wref], w=[tmp])
                        self.vop(lambda e, d=CL, t=tmp: e.tensor_scalar(
                            d, t, 0.0, None, AL.min), r=[tmp], w=[CL])
                        self.sop(lambda e, d=CL: e.activation(
                            d, d, AF.Exp, bias=0.0, scale=1.0), r=[CL], w=[CL])
                        self.vop(lambda e, d=CL, C=Csl: e.tensor_tensor(
                            d, d, C, AL.mult), r=[CL, Csl], w=[CL])
                        self.vop(lambda e, d=BL, t=tmp: e.tensor_scalar(
                            d, t, 0.0, None, AL.max), r=[tmp], w=[BL])
                        self.sop(lambda e, d=BL: e.activation(
                            d, d, AF.Exp, bias=0.0, scale=-1.0), r=[BL], w=[BL])
                        self.vop(lambda e, d=BL, B=Bsl: e.tensor_tensor(
                            d, d, B, AL.mult), r=[BL, Bsl], w=[BL])
                        for bi in range(G // L):
                            a = bi * L
                            piv = a + L // 2
                            sft = a % 32 if L <= 16 else 0
                            if L >= 32:
                                sft = 0
                            tile = sft // 8
                            for h in range(2):
                                hs = slice(h * 64, h * 64 + 64)
                                ob = a - sft
                                tp = (h * 64, (ob // 32) * 32) if ob >= 96 \
                                    else None
                                self.mm(bkM[h][ob:piv - sft,
                                               tile * 128 + piv:tile * 128 + a + L],
                                        BL[hs, a:piv], CL[hs, piv:a + L],
                                        start=False, stop=True, tp=tp,
                                        skip_group=True)
                    M0 = AR(a_M, 512)
                    M1 = AR(a_M + 512, 512)
                    for h, Mh in ((0, M0), (1, M1)):
                        self.vop(lambda e, d=Mh, p=bkM[h]:
                                 e.tensor_copy(d, p[:, 0:512]),
                                 r=[bkM[h][:, 0:512]], w=[Mh])
                    bkY0 = self.bank()
                    bkY1 = self.bank()
                    bkS0 = self.bank()
                    bkS1 = self.bank()
                    for h, (bkY, Mh) in enumerate(((bkY0, M0), (bkY1, M1))):
                        ob = 0 if h == 0 else 64
                        first = True
                        for s in (0, 8, 16, 24):
                            self.mm(bkY[ob:ob + 64, 0:G],
                                    xTs[s][:, h * 64:h * 64 + 64],
                                    Mh[:, (s // 8) * 128:(s // 8) * 128 + 128],
                                    start=first, stop=False)
                            first = False
                        self.mm(bkY[ob:ob + 64, 0:G],
                                Hsl[ob:ob + 64, ob:ob + 64], C2[ob:ob + 64, :],
                                start=False, stop=True)
                    self.mm(bkS0[0:64, 0:64], BT[:, 0:64], xTs[0][:, 0:64],
                            start=True, stop=True)
                    self.mm(bkS1[64:128, 0:64], BT[:, 64:128], xTs[0][:, 64:128],
                            start=True, stop=True)
                    ysl = AR(a_y + hp * TC + cc, G)
                    dc = self.cst["dcol"][:, hp:hp + 1]
                    self.vop(lambda e, y=ysl, x=xh_sl, p=bkY0, d=dc:
                             e.scalar_tensor_tensor(y[0:64, :], x[0:64, :],
                                                    d[0:64, :], p[0:64, 0:G],
                                                    AL.mult, AL.add),
                             r=[xh_sl[0:64, :], dc[0:64, :], bkY0[0:64, 0:G]],
                             w=[ysl[0:64, :]])
                    self.vop(lambda e, y=ysl, x=xh_sl, p=bkY1, d=dc:
                             e.scalar_tensor_tensor(y[64:128, :], x[64:128, :],
                                                    d[64:128, :], p[64:128, 0:G],
                                                    AL.mult, AL.add),
                             r=[xh_sl[64:128, :], dc[64:128, :],
                                bkY1[64:128, 0:G]],
                             w=[ysl[64:128, :]])
                    self.vop(lambda e, H=Hsl, c2=cf, p=bkS0:
                             e.scalar_tensor_tensor(H[0:64, 0:64], H[0:64, 0:64],
                                                    c2[0:64, 2:3], p[0:64, 0:64],
                                                    AL.mult, AL.add),
                             r=[Hsl[0:64, 0:64], cf[0:64, 2:3], bkS0[0:64, 0:64]],
                             w=[Hsl[0:64, 0:64]])
                    self.vop(lambda e, H=Hsl, c2=cf, p=bkS1:
                             e.scalar_tensor_tensor(H[64:128, 64:128],
                                                    H[64:128, 64:128],
                                                    c2[64:128, 2:3],
                                                    p[64:128, 0:64],
                                                    AL.mult, AL.add),
                             r=[Hsl[64:128, 64:128], cf[64:128, 2:3],
                                bkS1[64:128, 0:64]],
                             w=[Hsl[64:128, 64:128]])

                    bkB = self.bank()
                    band = AR(a_band, G)
                    for dl in range(4):
                        if dl == 0:
                            self.vop(lambda e, d=band, C=Csl, B=Bsl:
                                     e.tensor_tensor(d, C, B, AL.mult),
                                     r=[Csl, Bsl], w=[band])
                        else:
                            wshift = AR(a_W + hp * TC + cc - dl, G)
                            bshift = AR(a_B + hp * TC + cc - dl, G)
                            self.vop(lambda e, d=band, W=Wsl, ws=wshift:
                                     e.tensor_tensor(d, W, ws, AL.subtract),
                                     r=[Wsl, wshift], w=[band])
                            self.vop(lambda e, d=band: e.tensor_scalar(
                                d, d, 0.0, None, AL.min), r=[band], w=[band])
                            self.sop(lambda e, d=band: e.activation(
                                d, d, AF.Exp, bias=0.0, scale=1.0),
                                r=[band], w=[band])
                            self.vop(lambda e, d=band, C=Csl:
                                     e.tensor_tensor(d, d, C, AL.mult),
                                     r=[band, Csl], w=[band])
                            self.vop(lambda e, d=band, bs=bshift:
                                     e.tensor_tensor(d, d, bs, AL.mult),
                                     r=[band, bshift], w=[band])
                        self.mm(bkB[:, dl * 2:dl * 2 + 2], band,
                                self.cst["ones2"][:], start=True, stop=True)
                    bev = AR(a_mrow, 16)
                    self.vop(lambda e, d=bev, p=bkB: e.tensor_copy(d[:, 0:8],
                                                                   p[:, 0:8]),
                             r=[bkB[:, 0:8]], w=[bev[:, 0:8]])
                    bkBT = self.bank()
                    self.transpose(bkBT[0:8, 0:G], bev[:, 0:8])
                    brow = AR(a_band, G)
                    self.vop(lambda e, d=brow, p=bkBT:
                             e.tensor_copy(d[0:8, :], p[0:8, 0:G]),
                             r=[bkBT[0:8, 0:G]], w=[brow[0:8, :]])
                    self.vop(lambda e, d=brow: e.tensor_tensor(
                        d[0:8, :], d[0:8, :], self.cst["m4"][0:8, :], AL.mult),
                        r=[brow[0:8, :], self.cst["m4"][0:8, :]],
                        w=[brow[0:8, :]])
                    for dl in range(4):
                        bkbc = self.bank()
                        self.mm(bkbc[:, 0:G],
                                self.cst["bsel"][0:8, dl * 128:(dl + 1) * 128],
                                brow[0:8, :], start=True, stop=True)
                        tb_ = AR(a_tmp + hp * TC + cc, G)
                        n = G - dl
                        self.vop(lambda e, t=tb_, x=xh_sl, p=bkbc, d=dl, n=n:
                                 e.tensor_tensor(t[:, d:G], x[:, 0:n],
                                                 p[:, d:G], AL.mult),
                                 r=[xh_sl[:, 0:n], bkbc[:, dl:G]],
                                 w=[tb_[:, dl:G]])
                        self.vop(lambda e, y=ysl, t=tb_, d=dl:
                                 e.tensor_tensor(y[:, d:G], y[:, d:G],
                                                 t[:, d:G], AL.add),
                                 r=[ysl[:, dl:G], tb_[:, dl:G]],
                                 w=[ysl[:, dl:G]])
            for ch in range(16):
                ya = AR(a_y + ch * TC, TC)
                za = AR(a_z + ch * TC, TC)
                self.vop(lambda e, y=ya, z=za: e.tensor_tensor(y, y, z, AL.mult),
                         r=[ya, za], w=[ya])

            for m in range(8):
                bk = self.bank()
                pairs = []
                for k in range(16):
                    wt = self.wtile()
                    pairs.append((wt, AR(a_y + k * TC, TC)))
                self.pe_group(bk[:, 0:TC], pairs)
                xa = AR(a_xacc, TC)
                self.dma("scalar", "s_ins", xa,
                         self.dr["x"][m * 128:(m + 1) * 128, t0:t0 + TC])
                self.vop(lambda e, p=bk, x=xa:
                         e.tensor_tensor(x, x, p[:, 0:TC], AL.add),
                         r=[xa, bk[:, 0:TC]], w=[xa])
                self.dma("gpsimd", "s_out",
                         self.dr["acc"][m * 128:(m + 1) * 128, t0:t0 + TC], xa)

    def p2_ffn(self):
        AR = self.ar
        f_ov = 5632
        self.f_masks = 0
        self.f_tab = 1536
        self.q_off = f_ov
        self.k_off = f_ov + 512
        self.v_off = f_ov + 1024
        self.t1_off = f_ov + 1536
        self.t2_off = f_ov + 2048
        self.kbuf_off = f_ov + 2560
        self.vbuf_off = f_ov + 3328
        self.ksave_off = f_ov + 4096
        self.vsave_off = f_ov + 6144
        self.vT_off = f_ov + 8192
        self.e_off = f_ov + 8960
        self.eT_off = f_ov + 9984
        self.av_off = f_ov + 11008
        self.sqs_off = f_ov + 15104
        self.xn2_off = f_ov
        self.h2_off = f_ov + 4096
        self.sq2_off = f_ov + 4096
        self.g1_off = f_ov + 15360
        self.lrows_off = 22016
        self.vrow_off = 22528
        self.zc_off = 22528
        self.f_defer = 21504
        self.epsc2 = 21512

        self.dma("gpsimd", "s_ing", AR(self.f_masks, 1536), self.dr["c_masks"][:, :])
        ep = AR(self.epsc2, 1)
        self.vop(lambda e: e.memset(ep[0:1, :], 1e-5), w=[ep[0:1, :]])
        for l in range(self.cfg["layers"]):
            self.attn_subphase(l)
            self.ffn_subphase(l)

    def attn_subphase(self, l):
        AR = self.ar
        defer_live = False
        for tb in range(self.cfg.get("tb1") or NTB):
            t0 = tb * TB
            par = tb % 2
            tabs = AR(self.f_tab + par * 2048, 2048)
            for j in range(4):
                self.dma("scalar", "s_ins", tabs[:, j * TB:(j + 1) * TB],
                         self.dr["c_tables"][:, j * T + t0:j * T + t0 + TB])
            for hp in range(AHP):
                self.attn_hp(l, tb, hp, tabs)
            for m in range(8):
                bk = self.bank()
                pairs = []
                for khp in range(8):
                    wt = self.wtile()
                    pairs.append((wt, AR(self.av_off + khp * TB, TB)))
                self.pe_group(bk[:, 0:TB], pairs)
                if m == 0 and defer_live:
                    self.apply_defer(t0 - 1)
                    defer_live = False
                dst = self.ucol(m, t0, TB - 1)
                self.vop(lambda e, d=dst, p=bk:
                         e.tensor_tensor(d, d, p[:, 0:TB - 1], AL.add),
                         r=[dst, bk[:, 0:TB - 1]], w=[dst])
                dbuf = AR(self.f_defer, 8)
                lastu = self.ucol(m, t0 + TB - 1, 1)
                self.vop(lambda e, d=dbuf, p=bk, m_=m, lu=lastu: e.tensor_tensor(
                    d[:, m_:m_ + 1], lu, p[:, TB - 1:TB], AL.add),
                    r=[lastu, bk[:, TB - 1:TB]], w=[dbuf[:, m:m + 1]])
            defer_live = True
        self.apply_defer(T - 1)

    def apply_defer(self, tcol):
        dbuf = self.ar(self.f_defer, 8)
        for ch in range(NCH):
            dst = self.ucol(ch, tcol, 1)
            self.vop(lambda e, c=ch, d=dbuf, ds=dst: e.tensor_copy(
                ds, d[:, c:c + 1]), r=[dbuf[:, ch:ch + 1]], w=[dst])

    def attn_hp(self, l, tb, hp, tabs):
        AR = self.ar
        cst = self.cst
        t0 = tb * TB
        qb = AR(self.q_off, TB)
        kb = AR(self.k_off, TB)
        kbuf = AR(self.kbuf_off, 768)
        vbuf = AR(self.vbuf_off, 768)
        ksave = AR(self.ksave_off + hp * 256, 256)
        vsave = AR(self.vsave_off + hp * 256, 256)
        vT = AR(self.vT_off, 768)
        t1 = AR(self.t1_off, TB)
        t2 = AR(self.t2_off, TB)
        sqs = AR(self.sqs_off, TB)
        lrows = AR(self.lrows_off, TB)
        zcols = AR(self.zc_off, 8)

        bq = self.bank()
        bkk = self.bank()
        bv = self.bank()
        for role, bk in ((0, bq), (1, bkk), (2, bv)):
            pairs = []
            for k in range(8):
                wt = self.wtile()
                pairs.append((wt, self.ucol(k, t0, TB, shift=True)))
            self.pe_group(bk[:, 0:TB], pairs)
        self.vop(lambda e, p=bq: e.tensor_copy(qb, p[:, 0:TB]),
                 r=[bq[:, 0:TB]], w=[qb])
        self.vop(lambda e, p=bkk: e.tensor_copy(kb, p[:, 0:TB]),
                 r=[bkk[:, 0:TB]], w=[kb])
        self.vop(lambda e: e.tensor_copy(kbuf[:, 0:256], ksave),
                 r=[ksave], w=[kbuf[:, 0:256]])
        self.vop(lambda e: e.tensor_copy(vbuf[:, 0:256], vsave),
                 r=[vsave], w=[vbuf[:, 0:256]])
        self.sop(lambda e, p=bv: e.copy(vbuf[:, 256:768], p[:, 0:TB]),
                 r=[bv[:, 0:TB]], w=[vbuf[:, 256:768]])

        for which in (0, 1):
            buf = qb if which == 0 else kb
            sc = cst["qs"] if which == 0 else cst["ks"]
            bn = self.bank()
            bb = self.bank()
            self.sop(lambda e, b_=buf, d=sqs: e.activation(
                d, b_, AF.Square, bias=0.0, scale=1.0), r=[buf], w=[sqs])
            self.mm(bn[0:2, 0:TB], cst["ones2"][:], sqs, start=True, stop=True)
            self.sop(lambda e, p=bn, d=lrows: e.activation(
                d[0:2, :], p[0:2, 0:TB], AF.Sqrt, bias=0.0, scale=1.0),
                r=[bn[0:2, 0:TB]], w=[lrows[0:2, :]])
            self.vop(lambda e, d=lrows: e.tensor_scalar(
                d[0:2, :], d[0:2, :], 1e-12, None, AL.max),
                r=[lrows[0:2, :]], w=[lrows[0:2, :]])
            self.vop(lambda e, d=lrows: e.reciprocal(d[0:2, :], d[0:2, :]),
                     r=[lrows[0:2, :]], w=[lrows[0:2, :]])
            self.mm(bb[:, 0:TB], cst["bc2"][:], lrows[0:2, :], start=True,
                    stop=True)
            scl = sc[:, l:l + 1]
            self.vop(lambda e, b_=buf, s=scl, p=bb:
                     e.scalar_tensor_tensor(b_, b_, s, p[:, 0:TB],
                                            AL.mult, AL.mult),
                     r=[buf, scl, bb[:, 0:TB]], w=[buf])
            bp = self.bank()
            self.mm(bp[:, 0:TB], cst["prot"][:], buf, start=True, stop=True)
            ctab = tabs[:, (0 if which == 0 else 2 * TB):
                        (TB if which == 0 else 3 * TB)]
            stab = tabs[:, (TB if which == 0 else 3 * TB):
                        (2 * TB if which == 0 else 4 * TB)]
            self.gop(lambda e, b_=buf, c=ctab, d=t1:
                     e.tensor_tensor(d, b_, c, AL.mult),
                     r=[buf, ctab], w=[t1])
            self.vop(lambda e, p=bp, s=stab, d=t2:
                     e.tensor_tensor(d, p[:, 0:TB], s, AL.mult),
                     r=[bp[:, 0:TB], stab], w=[t2])
            if which == 0:
                self.vop(lambda e, b_=buf: e.tensor_tensor(b_, t1, t2, AL.add),
                         r=[t1, t2], w=[buf])
            else:
                dstk = kbuf[:, 256:768]
                self.vop(lambda e, d=dstk: e.tensor_tensor(d, t1, t2, AL.add),
                         r=[t1, t2], w=[dstk])
        self.vop(lambda e: e.tensor_copy(ksave, kbuf[:, 512:768]),
                 r=[kbuf[:, 512:768]], w=[ksave])
        self.vop(lambda e: e.tensor_copy(vsave, vbuf[:, 512:768]),
                 r=[vbuf[:, 512:768]], w=[vsave])

        jlo = 2 if tb == 0 else 0
        for jc in range(jlo, 6):
            bkt = self.bank()
            self.transpose(bkt[:, 0:128], vbuf[:, jc * 128:(jc + 1) * 128])
            dst = vT[:, jc * 128:(jc + 1) * 128]
            self.vop(lambda e, d=dst, p=bkt: e.tensor_copy(d, p[:, 0:128]),
                     r=[bkt[:, 0:128]], w=[dst])

        for head in range(2):
            hrows = slice(head * 64, head * 64 + 64)
            hb = head * 64
            for wb in range(2):
                w_abs = tb * 2 + wb
                first = (w_abs == 0)
                nk = 256 if first else 512
                koff = 256 if first else wb * 256
                bs = [self.bank(), self.bank()]
                for qt in (0, 1):
                    self.mm(bs[qt][:, 0:nk],
                            qb[hrows, wb * 256 + qt * 128:
                               wb * 256 + qt * 128 + 128],
                            kbuf[hrows, koff:koff + nk],
                            start=True, stop=True)
                for qt in (0, 1):
                    moff = self.f_masks + (1024 + qt * 256 if first else qt * 512)
                    esl = AR(self.e_off + qt * TB, nk)
                    msk = AR(moff, nk)
                    bsq = bs[qt]
                    self.vop(lambda e, d=esl, p=bsq, m=msk, n=nk:
                             e.tensor_tensor(d, p[:, 0:n], m, AL.add),
                             r=[bsq[:, 0:nk], msk], w=[esl])
                    self.vop(lambda e, d=esl, z=zcols, q=qt:
                             e.reduce_max(z[:, q:q + 1], d, axis=AX, negate=True),
                             r=[esl], w=[zcols[:, qt:qt + 1]])
                for qt in (0, 1):
                    esl = AR(self.e_off + qt * TB, nk)
                    self.sop(lambda e, d=esl, z=zcols, q=qt: e.activation(
                        d, d, AF.Exp, bias=z[:, q:q + 1], scale=1.0,
                        accum_out=z[:, 2 + q:3 + q]),
                        r=[esl, zcols[:, qt:qt + 1]],
                        w=[esl, zcols[:, 2 + qt:3 + qt]])
                self.vop(lambda e, z=zcols: e.reciprocal(z[:, 4:6], z[:, 2:4]),
                         r=[zcols[:, 2:4]], w=[zcols[:, 4:6]])
                for qt in (0, 1):
                    esl = AR(self.e_off + qt * TB, nk)
                    self.vop(lambda e, d=esl, z=zcols, q=qt:
                             e.tensor_scalar(d, d, z[:, 4 + q:5 + q], None,
                                             AL.mult),
                             r=[esl, zcols[:, 4 + qt:5 + qt]], w=[esl])
                njc = nk // 128
                pairs = []
                for qt in range(2):
                    for jc in range(njc):
                        if not first:
                            if qt == 0 and jc == 3:
                                continue
                            if qt == 1 and jc == 0:
                                continue
                        pairs.append((qt, jc))
                for qt, jc in pairs:
                    bkt = self.bank()
                    src = AR(self.e_off + qt * TB + jc * 128, 128)
                    self.transpose(bkt[:, 0:128], src)
                    dst = AR(self.eT_off + jc * 256 + qt * 128, 128)
                    self.vop(lambda e, d=dst, p=bkt: e.tensor_copy(d, p[:, 0:128]),
                             r=[bkt[:, 0:128]], w=[dst])
                bav = self.bank()
                av_pairs = []
                for jc in range(njc):
                    vjc = (koff // 128) + jc
                    if first:
                        qlo, qn = 0, 256
                    elif jc == 0:
                        qlo, qn = 0, 128
                    elif jc == 3:
                        qlo, qn = 128, 128
                    else:
                        qlo, qn = 0, 256
                    av_pairs.append((vjc, jc, qlo, qn))
                av_pairs.sort(key=lambda t_: -t_[3])
                for idx, (vjc, jc, qlo, qn) in enumerate(av_pairs):
                    self.mm(bav[hb:hb + 64, qlo:qlo + qn],
                            vT[:, vjc * 128 + hb:vjc * 128 + hb + 64],
                            AR(self.eT_off + jc * 256 + qlo, qn),
                            start=(idx == 0), stop=(idx == len(av_pairs) - 1))
                dst = AR(self.av_off + hp * TB + wb * 256, 256)[hb:hb + 64, :]
                self.vop(lambda e, d=dst, p=bav, h=hb:
                         e.tensor_copy(d, p[h:h + 64, 0:256]),
                         r=[bav[hb:hb + 64, 0:256]], w=[dst])

    def ffn_subphase(self, l):
        AR = self.ar
        cst = self.cst
        defer_live = False
        for tb in range(NTB):
            t0 = tb * TB
            mrow = AR(self.lrows_off, TB)
            vrow = AR(self.vrow_off, TB)
            sqa = AR(self.sq2_off, NCH * TB)
            bmu, brs = self.ln_stats(
                lambda ch: self.ucol(ch, t0, TB, shift=True), mrow, vrow, sqa,
                AR(self.epsc2, 1)[0:1, 0:1])
            for ch in range(NCH):
                xsl = AR(self.xn2_off + ch * TB, TB)
                src = self.ucol(ch, t0, TB, shift=True)
                self.vop(lambda e, d=xsl, s=src, mu=bmu: e.tensor_tensor(
                    d, s, mu[:, 0:TB], AL.subtract),
                    r=[src, bmu[:, 0:TB]], w=[xsl])
                self.vop(lambda e, d=xsl, rs=brs: e.tensor_tensor(
                    d, d, rs[:, 0:TB], AL.mult),
                    r=[xsl, brs[:, 0:TB]], w=[xsl])
                gcol = cst["lng"][:, l * 8 + ch:l * 8 + ch + 1]
                bcol = cst["lnb"][:, l * 8 + ch:l * 8 + ch + 1]
                self.vop(lambda e, d=xsl, g=gcol, bb=bcol:
                         e.tensor_scalar(d, d, g, bb, AL.mult, AL.add),
                         r=[xsl, gcol, bcol], w=[xsl])
            for k2 in range(K2N):
                ba = self.bank()
                bg = self.bank()
                for half, bk in ((0, ba), (1, bg)):
                    pairs = []
                    for k in range(8):
                        wt = self.wtile()
                        pairs.append((wt, AR(self.xn2_off + k * TB, TB)))
                    self.pe_group(bk[:, 0:TB], pairs)
                bcol_a = cst["b1"][:, (l * K2N + k2) * 2:(l * K2N + k2) * 2 + 1]
                bcol_g = cst["b1"][:, (l * K2N + k2) * 2 + 1:
                                   (l * K2N + k2) * 2 + 2]
                gl = AR(self.g1_off, TB)
                self.sop(lambda e, p=bg, b_=bcol_g, d=gl: e.activation(
                    d, p[:, 0:TB], AF.Gelu, bias=b_, scale=1.0),
                    r=[bg[:, 0:TB], bcol_g], w=[gl])
                h2sl = AR(self.h2_off + k2 * TB, TB)
                self.vop(lambda e, d=h2sl, p=ba, b_=bcol_a, g=gl:
                         e.scalar_tensor_tensor(d, p[:, 0:TB], b_, g,
                                                AL.add, AL.mult),
                         r=[ba[:, 0:TB], bcol_a, gl], w=[h2sl])
            bks = [self.bank() for _ in range(8)]
            for k2 in range(K2N):
                for m in range(8):
                    wt = self.wtile()
                    self.mm(bks[m][:, 0:TB], wt, AR(self.h2_off + k2 * TB, TB),
                            start=(k2 == 0), stop=(k2 == K2N - 1))
            if defer_live:
                self.apply_defer(t0 - 1)
                defer_live = False
            for m in range(8):
                dst = self.ucol(m, t0, TB - 1)
                b2c = cst["b2"][:, l * 8 + m:l * 8 + m + 1]
                bkm = bks[m]
                self.vop(lambda e, d=dst, p=bkm, b_=b2c:
                         e.scalar_tensor_tensor(d, p[:, 0:TB - 1], b_, d,
                                                AL.add, AL.add),
                         r=[bkm[:, 0:TB - 1], b2c, dst], w=[dst])
                dbuf = AR(self.f_defer, 8)
                lastu = self.ucol(m, t0 + TB - 1, 1)
                self.vop(lambda e, d=dbuf, p=bkm, b_=b2c, m_=m, lu=lastu:
                         e.scalar_tensor_tensor(d[:, m_:m_ + 1],
                                                p[:, TB - 1:TB], b_, lu,
                                                AL.add, AL.add),
                         r=[bkm[:, TB - 1:TB], b2c, lastu],
                         w=[dbuf[:, m:m + 1]])
            defer_live = True
        self.apply_defer(T - 1)

    def p3_final(self):
        ostage = self.ar(0, TB)
        astage = self.ar(TB, TB)
        src = self.dr["acc"] if (self.cfg["mamba"] and not self.cfg["ln_only"]) \
            else self.dr["x"]
        for ch in range(NCH):
            for tt in range(NTB):
                self.dma("scalar", "s_ins", astage,
                         src[ch * 128:(ch + 1) * 128, tt * TB:(tt + 1) * TB])
                usl = self.ucol(ch, tt * TB, TB)
                self.vop(lambda e, u_=usl: e.tensor_tensor(
                    ostage, astage, u_, AL.add), r=[astage, usl], w=[ostage])
                self.dma("gpsimd", "s_out",
                         self.dr["out"][ch * 128:(ch + 1) * 128,
                                        tt * TB:(tt + 1) * TB], ostage)
        for j in range(self.QK["s_out"]):
            s = f"s_out{j}"
            self.streams["gpsimd"].append(
                lambda e, h=self.sems[s], v=self.counts[s]: e.wait_ge(h, v))


def kernel(**inputs):
    cfg = dict(CFG)
    host = host_prepare(inputs)
    shapes = {k: v.shape for k, v in host.items()}
    x = np.asarray(inputs["L0_attn_x"])
    nc = Kern(cfg, shapes).build()
    in_maps = []
    for core in range(8):
        m = {"x": _f32(x[core].T)}
        m.update(host)
        in_maps.append(m)
    res = run_bass_kernel_spmd(nc, in_maps, core_ids=list(range(8)))
    outs = [res.results[i]["out"].T for i in range(8)]
    return np.ascontiguousarray(np.stack(outs, axis=0), dtype=np.float32)
